# version 32
# baseline (speedup 1.0000x reference)
import numpy as np
import ml_dtypes
import concourse.bass as bass
import concourse.mybir as mybir
from concourse.bass_utils import run_bass_kernel_spmd

F32 = mybir.dt.float32
BF16 = mybir.dt.bfloat16
I8 = mybir.dt.int8
AF = mybir.ActivationFunctionType
ALU = mybir.AluOpType
AX = mybir.AxisListType
BF = ml_dtypes.bfloat16

D = 2048
DKV = 512
NH = 8
NQ = 4
DH = 64
INTER = 8192
EPS = 1e-5
RD = 1.0 / 2048.0

ENGS = ("t", "v", "s", "y", "g")
ENG_SEM = {"t": "t", "v": "v", "s": "s", "g": "gc"}
DMA_STREAMS = ("c", "m", "wv", "xs0", "xs1", "a0", "a1", "a2", "a3", "sg",
               "o0", "o1", "o2", "o3", "wg0", "wg1", "wu0", "wu1",
               "wd0", "wd1", "wd2", "wd3", "wd4", "wd5", "r0", "r1")
ALL_SEMS = ("t", "v", "s", "gc") + DMA_STREAMS
STEP = {k: (1 if k in ("t", "v", "s", "gc") else 16) for k in ALL_SEMS}


class Plan:
    def __init__(self):
        self.ops = {e: [] for e in ENGS}
        self.cnt = {k: 0 for k in ALL_SEMS}
        self.lastw = {e: {} for e in ENGS}

    def do(self, e, fn, waits=(), inc=True, stream=None, relax=False):
        sname = stream if stream is not None else ENG_SEM.get(e)
        if stream is None and e != "t" and not relax and self.cnt[sname] > 0:
            self.ops[e].append(("w", sname, self.cnt[sname]))
            self.lastw[e][sname] = self.cnt[sname]
        for tok in waits:
            if not tok:
                continue
            ws, wv = tok
            if self.lastw[e].get(ws, 0) >= wv:
                continue
            self.lastw[e][ws] = wv
            self.ops[e].append(("w", ws, wv))
        self.cnt[sname] += STEP[sname]
        self.ops[e].append(("i", fn, sname))
        return (sname, self.cnt[sname])

    def raw_wait(self, e, sname, val):
        self.ops[e].append(("w", sname, val))


def build_nc():
    nc = bass.Bass("TRN2", target_bir_lowering=False)

    x_d = nc.dram_tensor("x", [2048, D], F32, kind="ExternalInput").ap()
    wqt_d = nc.dram_tensor("wqt", [16, 128, 2048], F32, kind="ExternalInput").ap()
    wkt_d = nc.dram_tensor("wkt", [4, 128, 2048], F32, kind="ExternalInput").ap()
    wvt_d = nc.dram_tensor("wvt", [128, 8192], F32, kind="ExternalInput").ap()
    mem_d = nc.dram_tensor("memi", [128, 2048], F32, kind="ExternalInput").ap()
    mnorm_d = nc.dram_tensor("mnorm", [128, 4], F32, kind="ExternalInput").ap()
    mw_d = nc.dram_tensor("mw", [128, 1], F32, kind="ExternalInput").ap()
    wg_d = nc.dram_tensor("wg8", [64, 128, 2048], I8, kind="ExternalInput").ap()
    wu_d = nc.dram_tensor("wu8", [64, 128, 2048], I8, kind="ExternalInput").ap()
    wd_d = nc.dram_tensor("wd8", [INTER, D], I8, kind="ExternalInput").ap()
    sg_d = nc.dram_tensor("sg", [128, 64], F32, kind="ExternalInput").ap()
    su_d = nc.dram_tensor("su", [128, 64], F32, kind="ExternalInput").ap()
    sdb_d = nc.dram_tensor("sdb", [128, 2048], F32, kind="ExternalInput").ap()
    eyeb_d = nc.dram_tensor("eyeb", [128, 128], BF16, kind="ExternalInput").ap()
    eyef_d = nc.dram_tensor("eyef", [128, 128], F32, kind="ExternalInput").ap()
    blk2_d = nc.dram_tensor("blk2", [128, 2], BF16, kind="ExternalInput").ap()
    ones2_d = nc.dram_tensor("ones2", [2, 128], BF16, kind="ExternalInput").ap()
    mask2_d = nc.dram_tensor("mask2", [2, 32], F32, kind="ExternalInput").ap()
    onesf_d = nc.dram_tensor("onesf", [128, 1], F32, kind="ExternalInput").ap()
    y_d = nc.dram_tensor("y", [1024, D], F32, kind="ExternalOutput").ap()

    import contextlib
    es = contextlib.ExitStack()
    with es:
        sbt = lambda n, shp, dt: es.enter_context(nc.sbuf_tensor(n, shp, dt))
        eyeb = sbt("eyeb_s", [128, 128], BF16)
        eyef = sbt("eyef_s", [128, 128], F32)
        blk2 = sbt("blk2_s", [128, 2], BF16)
        ones2 = sbt("ones2_s", [2, 128], BF16)
        mask2 = sbt("mask2_s", [2, 32], F32)
        onesf = sbt("onesf_s", [128, 1], F32)
        mem_s = sbt("mem_s", [128, 2048], BF16)
        mnorm = sbt("mnorm_s", [128, 4], BF16)
        mw_s = sbt("mw_s", [128, 1], F32)
        wcol = sbt("wcol", [128, 2], F32)
        sg_s = sbt("sg_s", [128, 64], F32)
        su_s = sbt("su_s", [128, 64], F32)
        bias_s = sbt("bias_s", [128, 32], F32)
        stat = sbt("stat", [128, 16], F32)
        cq = sbt("cq", [2, 16], F32)
        ck = sbt("ck", [2, 4], F32)
        cm = sbt("cm", [2, 16], F32)
        cm2 = sbt("cm2", [2, 16], F32)
        nmax = sbt("nmax", [2, 2], F32)
        rhs2 = sbt("rhs2", [2, 32], BF16)
        denr = sbt("denr", [128, 16], F32)
        rcpd = sbt("rcpd", [128, 128], F32)
        rsrow = sbt("rsrow", [128, 1024], F32)
        rsall = sbt("rsall", [128, 512], F32)
        x1 = sbt("x1", [128, 4, 2048], F32)
        arena = sbt("arena", [128, 16384], BF16)
        q_fm = sbt("q_fm", [128, 16, 1024], BF16)
        k_fm = sbt("k_fm", [128, 4, 2048], BF16)
        v_tm = sbt("v_tm", [128, 16, 8, 66], BF16)
        warena = sbt("warena", [128, 12288], BF16)
        xstage = sbt("xstage", [128, 2, 2048], F32)
        lnstage = sbt("lnstage", [128, 2, 2048], BF16)
        ln2_fm = sbt("ln2_fm", [128, 16, 256], BF16)
        sqg = sbt("sqg", [128, 4, 512], BF16)
        sqs = sbt("sqs", [128, 1024], BF16)
        stg = sbt("stg", [128, 512], BF16)
        gtmp = sbt("gtmp", [128, 2, 256], BF16)
        utmp = sbt("utmp", [128, 2, 256], BF16)

        pst = lambda n, shp, dt: es.enter_context(nc.psum_tensor(n, shp, dt))
        pAB = pst("pAB", [128, 1024], F32)
        pCD = pst("pCD", [128, 1024], F32)
        pA = pAB[:, 0:512]
        pB = pAB[:, 512:1024]
        pC = pCD[:, 0:512]
        pD = pCD[:, 512:1024]
        pE = pst("pE", [128, 512], F32)
        pF = pst("pF", [128, 512], F32)
        pT0 = pst("pT0", [128, 1024], BF16)
        pT1 = pst("pT1", [128, 1024], BF16)

        sems = {k: es.enter_context(nc.semaphore("sem_" + k))
                for k in ALL_SEMS}

        mem4 = mem_s[:].rearrange("p (c f) -> p c f", f=512)
        A_ln = arena[:].rearrange("p (c t) -> p c t", t=1024)
        A_at = arena[:, 0:8192].rearrange("p (c s) -> p c s", s=512)
        A_et = arena[:, 8192:16384].rearrange("p (r s) -> p r s", s=512)
        A_etm = arena[:, 8192:16384].rearrange("p (r s) -> p r s", s=1024)
        A_mm = arena[:, 8192:16384].rearrange("p (c s) -> p c s", s=512)
        A_fu = arena[:].rearrange("p (i s) -> p i s", s=256)
        dtmp = xstage[:, 0, 0:512]
        wgb = [warena[:, b * 2048:(b + 1) * 2048]
               .rearrange("p (c m) -> p c m", m=128) for b in range(2)]
        wub = [warena[:, 4096 + b * 2048:4096 + (b + 1) * 2048]
               .rearrange("p (c m) -> p c m", m=128) for b in range(2)]
        wdb = [warena[:, 8192 + b * 2048:8192 + (b + 1) * 2048] for b in range(2)]
        wub_flat = [warena[:, 4096 + b * 2048:4096 + (b + 1) * 2048] for b in range(2)]
        wgb_flat = [warena[:, b * 2048:(b + 1) * 2048] for b in range(2)]
        wvb = warena[:, 4096:12288].rearrange("p (c f) -> p c f", f=512)

        P = Plan()
        qkv_banks = [pA, pB, pC, pD]

        tslot_banks = [pT0, pT1]
        tslot_war = [None] * 16
        tslot_idx = [0]

        def tr_slot():
            si = tslot_idx[0] % 2
            tslot_idx[0] += 1
            bank = tslot_banks[si]
            return si, bank[:, 0:128]

        t_consts = None
        for dst, src in ((eyeb, eyeb_d), (eyef, eyef_d), (blk2, blk2_d),
                         (ones2, ones2_d), (mask2, mask2_d), (onesf, onesf_d),
                         (sg_s, sg_d), (su_s, su_d), (mw_s, mw_d)):
            t_consts = P.do("y", lambda e, d=dst, s=src: e.dma_start(out=d[:], in_=s[:]), stream="c")
        t_mem = P.do("g", lambda e: e.dma_start(out=mem_s[:], in_=mem_d[:]), stream="m")
        t_mem = P.do("g", lambda e: e.dma_start(out=mnorm[:], in_=mnorm_d[:]), stream="m")
        t_wv = P.do("g", lambda e: e.dma_start(out=wvb[:], in_=wvt_d[:].rearrange("p (c f) -> p c f", f=512)), stream="wv")

        t_w1 = P.do("s", lambda e: e.activation(wcol[:, 0:1], mw_s[:], AF.Sigmoid),
                    waits=[t_consts], inc=True)
        t_w2 = P.do("v", lambda e: e.tensor_scalar(out=wcol[:, 1:2], in0=wcol[:, 0:1],
                    scalar1=-1.0, scalar2=1.0, op0=ALU.mult, op1=ALU.add),
                    waits=[t_w1], inc=True)
        t_vones = P.do("v", lambda e: e.memset(v_tm[:, :, :, 64:66], 1.0), inc=True)
        P.do("v", lambda e: e.memset(rsall[:], 1.0))
        P.do("v", lambda e: e.memset(ck[:], 0.0))

        def emit_ln_stats(src_ap, slot, waits_in, war_lnstage, sum_on_s=False):
            b = slot * 8
            s_ = lambda k: stat[:, b + k:b + k + 1]
            if sum_on_s:
                tv1 = P.do("s", lambda e: e.activation(x1[:, 0, :], src_ap,
                           AF.Copy, accum_out=s_(0)),
                           waits=waits_in + [war_lnstage], inc=True)
                P.lastj = tv1
            else:
                tv1 = P.do("v", lambda e: e.reduce_sum(s_(0), src_ap, axis=AX.X),
                           waits=waits_in + [war_lnstage], relax=True, inc=True)
            ts1 = P.do("s", lambda e: e.activation(lnstage[:, slot, :], src_ap,
                       AF.Square, accum_out=s_(1)),
                       waits=waits_in + [war_lnstage], relax=True, inc=True)
            ta = P.do("v", lambda e: e.tensor_scalar(out=s_(2), in0=s_(0),
                      scalar1=-RD, scalar2=None, op0=ALU.mult),
                      waits=[tv1, ts1], relax=True, inc=True)
            tb = P.do("v", lambda e: e.tensor_tensor(s_(3), s_(2), s_(2), ALU.mult),
                      waits=[ta], relax=True, inc=True)
            tc_ = P.do("v", lambda e: e.tensor_scalar(out=s_(4), in0=s_(1),
                       scalar1=RD, scalar2=EPS, op0=ALU.mult, op1=ALU.add),
                       waits=[ts1], relax=True, inc=True)
            td = P.do("v", lambda e: e.tensor_sub(s_(5), s_(4), s_(3)),
                      waits=[tb, tc_], relax=True, inc=True)
            tv2 = P.do("v", lambda e: e.reciprocal(s_(4), s_(5)),
                       waits=[td], relax=True, inc=True)
            ts2 = P.do("s", lambda e: e.activation(s_(6), s_(4), AF.Sqrt),
                       waits=[tv2], relax=True, inc=True)
            tv3 = P.do("v", lambda e: e.tensor_tensor(s_(7), s_(2), s_(6), ALU.mult),
                       waits=[ts2], relax=True, inc=True)
            return tv3

        def emit_ln_apply(src_ap, slot, tv3, eng="v"):
            b = slot * 8
            tap = P.do(eng, lambda e: e.tensor_scalar(out=lnstage[:, slot, :],
                       in0=src_ap, scalar1=stat[:, b + 6:b + 7],
                       scalar2=stat[:, b + 7:b + 8],
                       op0=ALU.mult, op1=ALU.add), waits=[tv3], inc=True)
            return tap

        def emit_ln(src_ap, slot, waits_in, war_lnstage):
            tv3 = emit_ln_stats(src_ap, slot, waits_in, war_lnstage)
            return emit_ln_apply(src_ap, slot, tv3)

        def emit_tr16(slot, dst_fn, t_apply, extra_read_war, bank_war=()):
            tv_last = None
            for c in range(16):
                si, ap = tr_slot()
                tt = P.do("t", lambda e, c=c, ap=ap: e.transpose(
                    ap, lnstage[:, slot, c * 128:(c + 1) * 128], eyeb[:]),
                    waits=[t_apply, tslot_war[si]] + list(bank_war), inc=True)
                tv = P.do("v", lambda e, c=c, ap=ap: e.tensor_copy(dst_fn(c), ap),
                          waits=[tt] + ([extra_read_war] if c < 2 else []),
                          relax=True, inc=True)
                tslot_war[si] = tv
                tv_last = tv
            return tv_last

        t_x = [None] * 16
        t_tile = [None] * 16
        war_xstage = [None, None]
        war_lnst = [None, None]
        t_half = [None, None]
        t_qkv_done = [None, None]
        qkv_war = [None, None, None, None]
        t_q_evac = {}
        t_k_evac = {}

        def norm_q_chunk(j):
            tvq = P.do("v", lambda e, j=j: e.tensor_tensor(
                sqs[:], q_fm[:, j, :], q_fm[:, j, :], ALU.mult),
                waits=[t_q_evac[j]], inc=True)
            for hl in range(2):
                tt = P.do("t", lambda e, hl=hl: e.matmul(
                    pF[0:2, 0:512], blk2[:], sqs[:, hl * 512:(hl + 1) * 512],
                    start=True, stop=True), waits=[tvq], inc=True)
                tvq = P.do("v", lambda e, hl=hl: e.reduce_max(
                    nmax[:, hl:hl + 1], pF[0:2, 0:512], axis=AX.X),
                    waits=[tt], inc=True)
            return P.do("v", lambda e, j=j: e.tensor_max(
                cq[:, j:j + 1], nmax[:, 0:1], nmax[:, 1:2]), inc=True)

        def norm_k_chunk(j2, H, t_evac):
            tok = t_evac
            for hl in range(2):
                tvq = P.do("v", lambda e, j2=j2, H=H, hl=hl: e.tensor_tensor(
                    sqs[:, 0:512], k_fm[:, j2, H * 1024 + hl * 512: H * 1024 + (hl + 1) * 512],
                    k_fm[:, j2, H * 1024 + hl * 512: H * 1024 + (hl + 1) * 512], ALU.mult),
                    waits=[tok], inc=True)
                tt = P.do("t", lambda e: e.matmul(
                    pF[0:2, 0:512], blk2[:], sqs[:, 0:512], start=True, stop=True),
                    waits=[tvq], inc=True)
                tvq = P.do("v", lambda e: e.reduce_max(
                    nmax[:, 0:1], pF[0:2, 0:512], axis=AX.X), waits=[tt], inc=True)
                tok = P.do("v", lambda e, j2=j2: e.tensor_max(
                    ck[:, j2:j2 + 1], ck[:, j2:j2 + 1], nmax[:, 0:1]), inc=True)
            return tok

        nb = [0]

        def qkv_group(lhs_fn, rhs_fn, evac_fn, nK, waits_mm):
            b = nb[0] % 4
            nb[0] += 1
            bank = qkv_banks[b]
            tt = None
            for c in range(nK):
                tt = P.do("t", lambda e, c=c, bank=bank: e.matmul(
                    bank[:, 0:512], lhs_fn(c), rhs_fn(c),
                    start=(c == 0), stop=(c == nK - 1)),
                    waits=(waits_mm + [qkv_war[b]]) if c == 0 else (), inc=(c == nK - 1))
            tv = P.do("v", lambda e, bank=bank: evac_fn(e, bank), waits=[tt], inc=True)
            qkv_war[b] = tv
            return tv

        war_wbuf = [t_wv, t_wv]
        t_wq = {}

        t_k0_done = None

        def emit_tile_tail(i):
            sl = i % 2
            i8 = i % 8
            extra = t_k0_done if i >= 8 else None
            tvl = emit_tr16(sl, lambda c, i8=i8: A_ln[:, c, i8 * 128:(i8 + 1) * 128],
                            t_apply[i], extra)
            war_lnst[sl] = tvl
            t_tile[i] = tvl
            qkv_group(
                lambda c, i8=i8: A_ln[:, c, i8 * 128:(i8 + 1) * 128],
                lambda c: wvb[:, c, 0:512],
                lambda e, bank, i=i: e.tensor_copy(
                    v_tm[:, i, :, 0:64],
                    bank[:, 0:512].rearrange("p (h d) -> p h d", d=64)),
                16, [t_wv, t_tile[i], t_vones])

        t_apply = [None] * 16
        for H in range(2):
            for i8 in range(8):
                i = H * 8 + i8
                sl = i % 2
                t_x[i] = P.do("y", lambda e, i=i, sl=sl: e.dma_start(
                    out=xstage[:, sl, :], in_=x_d[i * 128:(i + 1) * 128, :]),
                    waits=[war_xstage[sl]], stream="xs%d" % sl)
                tv3 = emit_ln_stats(xstage[:, sl, :], sl, [t_x[i]], war_lnst[sl],
                                    sum_on_s=True)
                if i8 > 0:
                    emit_tile_tail(i - 1)
                t_apply[i] = emit_ln_apply(xstage[:, sl, :], sl, tv3, eng="g")
                war_xstage[sl] = t_apply[i]
            emit_tile_tail(H * 8 + 7)
            t_half[H] = t_tile[H * 8 + 7]

            if H == 0:
                for j in range(16):
                    wb = j % 2
                    t_wq[j] = P.do("g", lambda e, j=j, wb=wb: e.dma_start(
                        out=wgb[wb][:], in_=wqt_d[j].rearrange("p (c m) -> p c m", m=128)),
                        waits=[war_wbuf[wb]], stream="wg%d" % wb)
                    for sc in range(2):
                        tv = qkv_group(
                            lambda c, j=j, wb=wb: wgb[wb][:, c, :],
                            lambda c, sc=sc: A_ln[:, c, sc * 512:(sc + 1) * 512],
                            lambda e, bank, j=j, sc=sc: e.tensor_copy(
                                q_fm[:, j, sc * 512:(sc + 1) * 512], bank[:, 0:512]),
                            16, [t_wq[j], t_tile[4 * sc + 3]])
                    t_q_evac[j] = tv
                    war_wbuf[wb] = ("t", P.cnt["t"])
                    norm_q_chunk(j)
            for j2 in range(4):
                wb = j2 % 2
                t_wk = P.do("g", lambda e, j2=j2, wb=wb: e.dma_start(
                    out=wgb[wb][:], in_=wkt_d[j2].rearrange("p (c m) -> p c m", m=128)),
                    waits=[war_wbuf[wb]], stream="wg%d" % wb)
                for tcH in range(2):
                    tv = qkv_group(
                        lambda c, wb=wb: wgb[wb][:, c, :],
                        lambda c, tcH=tcH: A_ln[:, c, tcH * 512:(tcH + 1) * 512],
                        lambda e, bank, j2=j2, H=H, tcH=tcH: e.tensor_copy(
                            k_fm[:, j2, H * 1024 + tcH * 512: H * 1024 + (tcH + 1) * 512],
                            bank[:, 0:512]),
                        16, [t_wk, t_tile[H * 8 + tcH * 4 + 3]])
                t_k_evac[(j2, H)] = tv
                war_wbuf[wb] = ("t", P.cnt["t"])
                norm_k_chunk(j2, H, tv)
            if H == 0:
                t_k0_done = ("t", P.cnt["t"])
            t_qkv_done[H] = ("t", P.cnt["t"])

        tvb = None
        for r4 in range(4):
            tvb = P.do("v", lambda e, r4=r4: e.tensor_tensor(
                cm[:, r4 * 4:(r4 + 1) * 4], cq[:, r4 * 4:(r4 + 1) * 4],
                ck[:, 0:4], ALU.mult), inc=True)
        tsb = P.do("s", lambda e: e.activation(cm2[:], cm[:], AF.Sqrt, scale=1.0 / 64.0),
                   waits=[tvb], inc=True)
        rhs2v = rhs2[:].rearrange("p (c u) -> p c u", u=2)
        cm2v = cm2[:].rearrange("a (b o) -> a b o", o=1)
        P.do("v", lambda e: e.tensor_copy(rhs2v[:, :, 0:1], cm2v[:]), waits=[tsb])
        P.do("v", lambda e: e.tensor_copy(rhs2v[:, :, 1:2], cm2v[:]))
        tvb = P.do("v", lambda e: e.tensor_tensor(rhs2[:], rhs2[:], mask2[:],
                   ALU.mult), waits=[t_consts])
        ttb = P.do("t", lambda e: e.matmul(pF[0:128, 32:64], ones2[:], rhs2[:],
                   start=True, stop=True), waits=[tvb], inc=True)
        t_bias = P.do("s", lambda e: e.activation(bias_s[:], pF[:, 32:64], AF.Copy,
                      scale=-1.0), waits=[ttb], inc=True)

        t_sdb = P.do("y", lambda e: e.dma_start(out=xstage[:, 1, :], in_=sdb_d[:]),
                     waits=[war_xstage[1]], stream="xs1")
        sdb = xstage[:, 1, :]

        war_arena_p2 = t_qkv_done[1]
        war_stg = None
        tvd1 = None
        war_sqg = None
        war_sqs_p2 = None
        war_downbanks = [None, None, None, None]
        war_gup = [None, None]
        war_lnst2 = list(war_lnst)
        war_ln2 = None
        war_wd = [t_qkv_done[1], t_qkv_done[1]]
        war_wu_reg = [None, None]
        war_wg_reg = [None, None]
        t_y = [None] * 4

        for p in range(2):
            if p == 0:
                war_sb_init = [[qkv_war[0], qkv_war[2]], [qkv_war[1], qkv_war[3]]]
                war_av = [None, t_bias]
                war_rsall = None
            else:
                war_sb_init = [[carry_tsg, war_downbanks[0], war_memb,
                                t_rcp_sub[3]],
                               [carry_tsu, war_downbanks[1], tvd1, t_rcp_sub[3]]]
                war_av = [war_downbanks[2], war_downbanks[3]]
                war_rsall = tt_rstr

            t_x1 = []
            for loc in range(4):
                t_x1.append(P.do("y", lambda e, p=p, loc=loc: e.dma_start(
                    out=x1[:, loc, :],
                    in_=x_d[p * 512 + loc * 128: p * 512 + (loc + 1) * 128, :]),
                    waits=[t_y[loc]] if p == 1 else
                          ([P.lastj] if loc == 0 else []), stream="a%d" % loc))

            t_exp = {}
            t_av = {}
            t_atA = {}
            t_atB = {}
            sbank = [pAB, pCD]
            avbank = [pE, pF]
            war_rsrow = [war_rsall, war_rsall]
            for i2 in range(16):
                prs = (2 * i2, 2 * i2 + 1)
                kc = (2 * i2) % 8 // 2
                for tc in range(22):
                    if tc < 16:
                        half = (tc % 2) * 512
                        w0 = [t_exp.get((prs[1], tc - 2)), t_bias,
                              war_arena_p2]
                        if i2 == 0 and tc < 2:
                            w0 += war_sb_init[tc % 2]
                        ttSA = P.do("t", lambda e, i2=i2, kc=kc, tc=tc, half=half, p=p:
                                    e.matmul(pAB[0:128, half:half + 512],
                                             k_fm[0:64, kc, tc * 128:(tc + 1) * 128],
                                             q_fm[0:64, i2, p * 512:(p + 1) * 512],
                                             start=True, stop=True,
                                             tile_position=(0, 0)),
                                    waits=w0, inc=True)
                        ttSB = P.do("t", lambda e, i2=i2, kc=kc, tc=tc, half=half, p=p:
                                    e.matmul(pCD[0:128, half:half + 512],
                                             k_fm[64:128, kc, tc * 128:(tc + 1) * 128],
                                             q_fm[64:128, i2, p * 512:(p + 1) * 512],
                                             start=True, stop=True,
                                             tile_position=(64, 0)),
                                    inc=True)
                        for st in range(2):
                            j = prs[st]
                            wslot = t_av.get((j, tc - 8))
                            if wslot is None and i2 > 0:
                                wslot = t_av.get((j - 2, tc + 8))
                            t_exp[(j, tc)] = P.do(
                                "s", lambda e, j=j, tc=tc, st=st, half=half:
                                e.activation(A_et[:, st * 8 + tc % 8, :],
                                             sbank[st][0:128, half:half + 512],
                                             AF.Exp, bias=bias_s[:, j:j + 1],
                                             scale=0.125),
                                waits=[ttSB, t_bias, wslot,
                                       war_arena_p2], relax=True, inc=True)
                    if tc >= 6:
                        tca = tc - 6
                        for st in range(2):
                            j = prs[st]
                            t_av[(j, tca)] = P.do(
                                "t", lambda e, j=j, tca=tca, st=st:
                                e.matmul(avbank[st][0:65, 0:512],
                                         v_tm[:, tca, j % 8, 0:65],
                                         A_et[:, st * 8 + tca % 8, :],
                                         start=(tca == 0), stop=(tca == 15)),
                                waits=[t_exp[(j, tca)],
                                       war_av[st] if tca == 0 else None],
                                inc=True)
                last_avA = t_av[(prs[0], 15)]
                tvdA = P.do("v", lambda e, i2=i2: e.tensor_copy(
                    arena[0:64, i2 * 512:(i2 + 1) * 512], pE[0:64, 0:512]),
                    waits=[last_avA, war_arena_p2], inc=True)
                t_atA[i2] = tvdA
                tvrA = P.do("v", lambda e: e.tensor_copy(
                    rsrow[64:65, 0:512], pE[64:65, 0:512]),
                    waits=[last_avA, war_rsrow[0]], relax=True, inc=True)
                war_av[0] = tvrA
                t_rsA = P.do("y", lambda e, jA=prs[0]: e.dma_start(
                    out=rsall[jA:jA + 1, 0:512], in_=rsrow[64:65, 0:512]),
                    waits=[tvrA, war_rsall], stream="r0")
                war_rsrow[0] = t_rsA
                last_avB = t_av[(prs[1], 15)]
                tvdB = P.do("v", lambda e: e.tensor_copy(
                    stg[0:64, 0:512], pF[0:64, 0:512]),
                    waits=[last_avB, war_stg], inc=True)
                t_atB[i2] = P.do("y", lambda e, i2=i2: e.dma_start(
                    out=arena[64:128, i2 * 512:(i2 + 1) * 512], in_=stg[0:64, 0:512]),
                    waits=[tvdB, war_arena_p2], stream="sg")
                war_stg = t_atB[i2]
                tvrB = P.do("v", lambda e: e.tensor_copy(
                    rsrow[64:65, 512:1024], pF[64:65, 0:512]),
                    waits=[last_avB, war_rsrow[1]], relax=True, inc=True)
                war_av[1] = tvrB
                t_rsB = P.do("y", lambda e, jB=prs[1]: e.dma_start(
                    out=rsall[jB:jB + 1, 0:512], in_=rsrow[64:65, 512:1024]),
                    waits=[tvrB, war_rsall], stream="r1")
                war_rsrow[1] = t_rsB

            war_exp_A = t_exp[(30, 15)]
            war_exp_B = t_exp[(31, 15)]
            rtslots = [pAB[:, 0:128], pAB[:, 512:640],
                       pCD[:, 0:128], pCD[:, 512:640]]
            t_rcp_sub = []
            tt_rstr = None
            for sub in range(4):
                tt_rstr = P.do("t", lambda e, sub=sub: e.transpose(
                    rtslots[sub],
                    rsall[:, sub * 128:(sub + 1) * 128], eyef[:]),
                    waits=[t_rsA, t_rsB, war_exp_A, war_exp_B], inc=True)
                t_rcp_sub.append(P.do("v", lambda e, sub=sub: e.reciprocal(
                    rcpd[:, sub * 32:(sub + 1) * 32],
                    rtslots[sub][:, 0:32]), waits=[tt_rstr], inc=True))
            t_rcpall = P.do("v", lambda e: e.tensor_scalar(
                out=rcpd[:], in0=rcpd[:], scalar1=wcol[:, 1:2], scalar2=None,
                op0=ALU.mult), waits=[t_rcp_sub[3], t_w2], inc=True)

            t_mm_evac = {}
            tt_den_last = None
            last_exp = t_exp[(31, 15)]
            last_avt = t_av[(31, 15)]
            war_memb = t_rcp_sub[3]
            for qg in range(4):
                tv_elu = None
                for ch in range(2):
                    sqsv = sqs[:].rearrange("p (a s) -> p a s", s=512)
                    tvm = P.do("v", lambda e, qg=qg, ch=ch, p=p: e.tensor_scalar(
                        out=sqsv[:], in0=q_fm[:, 4 * qg + 2 * ch: 4 * qg + 2 * ch + 2,
                                             p * 512:(p + 1) * 512],
                        scalar1=0.0, scalar2=None, op0=ALU.min),
                        waits=[war_sqs_p2], inc=True)
                    tse = P.do("s", lambda e, qg=qg, ch=ch: e.activation(
                        sqg[:, 2 * ch:2 * ch + 2, :], sqsv[:], AF.Exp),
                        waits=[tvm, war_sqg], inc=True)
                    tvm2 = P.do("v", lambda e, qg=qg, ch=ch, p=p: e.tensor_scalar(
                        out=sqsv[:], in0=q_fm[:, 4 * qg + 2 * ch: 4 * qg + 2 * ch + 2,
                                             p * 512:(p + 1) * 512],
                        scalar1=0.0, scalar2=None, op0=ALU.max),
                        waits=[tse], inc=True)
                    war_sqs_p2 = tvm2
                    tv_elu = P.do("v", lambda e, ch=ch: e.scalar_tensor_tensor(
                        out=sqg[:, 2 * ch:2 * ch + 2, :], in0=sqg[:, 2 * ch:2 * ch + 2, :],
                        scalar=-1.0, in1=sqsv[:], op0=ALU.add, op1=ALU.add),
                        waits=[tse, tvm2], inc=True)
                for fc in range(4):
                    ttm = None
                    for i4 in range(4):
                        ttm = P.do("t", lambda e, qg=qg, fc=fc, i4=i4: e.matmul(
                            pAB[0:128, 0:512], mem4[:, i4, fc * 128:(fc + 1) * 128],
                            sqg[:, i4, :], start=(i4 == 0), stop=(i4 == 3)),
                            waits=[tv_elu, t_mem, war_memb] if i4 == 0 else [],
                            inc=(i4 == 3))
                    war_memb = P.do("v", lambda e, qg=qg, fc=fc: e.tensor_copy(
                        A_mm[:, qg * 4 + fc, :], pAB[0:128, 0:512]),
                        waits=[ttm, last_avt, last_exp], inc=True)
                    t_mm_evac[qg * 4 + fc] = war_memb
                for sub in range(4):
                    for i4 in range(4):
                        tt_den_last = P.do("t", lambda e, qg=qg, sub=sub, i4=i4: e.matmul(
                            pAB[0:128, 512 + qg * 4 + sub: 512 + qg * 4 + sub + 1],
                            sqg[:, i4, sub * 128:(sub + 1) * 128],
                            mnorm[:, i4:i4 + 1], start=(i4 == 0), stop=(i4 == 3)),
                            waits=[tv_elu, t_mem, war_exp_A, tvd1,
                                   t_rcp_sub[1]] if (sub == 0 and i4 == 0) else [],
                            inc=(i4 == 3))
                war_sqg = tt_den_last
            tvd1 = P.do("v", lambda e: e.reciprocal(denr[:], pAB[:, 512:528]),
                        waits=[tt_den_last], inc=True)
            t_denr = P.do("v", lambda e: e.tensor_scalar(
                out=denr[:], in0=denr[:], scalar1=wcol[:, 0:1], scalar2=None,
                op0=ALU.mult), waits=[t_w2], inc=True)

            tv_stt = {}
            for sub in range(4):
                for c2 in range(16):
                    si, ap = tr_slot()
                    ttd = P.do("t", lambda e, c2=c2, sub=sub, ap=ap: e.transpose(
                        ap, arena[:, c2 * 512 + sub * 128: c2 * 512 + sub * 128 + 128],
                        eyeb[:]),
                        waits=[t_atA[c2], t_atB[c2], tslot_war[si]],
                        inc=True)
                    tva = P.do("v", lambda e, c2=c2, sub=sub, ap=ap: e.scalar_tensor_tensor(
                        out=x1[:, sub, c2 * 128: c2 * 128 + 64], in0=ap[:, 0:64],
                        scalar=rcpd[:, sub * 32 + 2 * c2: sub * 32 + 2 * c2 + 1],
                        in1=x1[:, sub, c2 * 128: c2 * 128 + 64],
                        op0=ALU.mult, op1=ALU.add),
                        waits=[ttd, t_rcpall, t_x1[sub], tv_stt.get((c2 - 1, sub))],
                        relax=True, inc=True)
                    tvb2 = P.do("v", lambda e, c2=c2, sub=sub, ap=ap: e.scalar_tensor_tensor(
                        out=x1[:, sub, c2 * 128 + 64: c2 * 128 + 128], in0=ap[:, 64:128],
                        scalar=rcpd[:, sub * 32 + 2 * c2 + 1: sub * 32 + 2 * c2 + 2],
                        in1=x1[:, sub, c2 * 128 + 64: c2 * 128 + 128],
                        op0=ALU.mult, op1=ALU.add),
                        waits=[ttd, t_rcpall, t_x1[sub], tv_stt.get((c2 - 1, sub))],
                        relax=True, inc=True)
                    tslot_war[si] = tvb2
                    si2, ap2 = tr_slot()
                    ttm2 = P.do("t", lambda e, c2=c2, sub=sub, ap2=ap2: e.transpose(
                        ap2,
                        arena[:, 8192 + c2 * 512 + sub * 128: 8192 + c2 * 512 + sub * 128 + 128],
                        eyeb[:]), waits=[t_mm_evac[c2], tslot_war[si2]],
                        inc=True)
                    tvm3 = P.do("v", lambda e, c2=c2, sub=sub, ap2=ap2: e.scalar_tensor_tensor(
                        out=x1[:, sub, c2 * 128: (c2 + 1) * 128], in0=ap2,
                        scalar=denr[:, (c2 // 4) * 4 + sub: (c2 // 4) * 4 + sub + 1],
                        in1=x1[:, sub, c2 * 128: (c2 + 1) * 128],
                        op0=ALU.mult, op1=ALU.add),
                        waits=[ttm2, t_denr], inc=True)
                    tslot_war[si2] = tvm3
                    tv_stt[(c2, sub)] = tvm3
            t_comb_done = ("v", P.cnt["v"])
            t_comb_done_t = ("t", P.cnt["t"])

            carry_tsg = t_comb_done_t
            carry_tsu = t_comb_done_t

            def emit_ln2(grp, war_ln2_tok):
                t_ln2 = None
                for st2 in range(2):
                    loc = grp * 2 + st2
                    sl = st2 % 2
                    tap = emit_ln(x1[:, loc, :], sl,
                                  [tv_stt[(15, loc)]], war_lnst2[sl])
                    tvl = emit_tr16(sl, lambda c, st2=st2: ln2_fm[:, c, st2 * 128:(st2 + 1) * 128],
                                    tap, war_ln2_tok)
                    war_lnst2[sl] = tvl
                    t_ln2 = tvl
                return t_ln2

            t_ln2_next = emit_ln2(0, war_ln2)
            dbanks = [pC, pD, pE, pF]
            for grp in range(2):
                t_ln2 = t_ln2_next
                ttdn0 = {}
                loc0 = grp * 2

                def emit_down0(ic, grp=grp):
                    sl = ic % 2
                    buf = wdb[sl]
                    t_wd = P.do("g", lambda e, ic=ic, buf=buf: e.dma_start(
                        out=buf[:], in_=wd_d[ic * 128:(ic + 1) * 128, :]),
                        waits=[ttdn0.get(ic - 2), war_wd[sl] if ic < 2 else None],
                        stream="wd%d" % sl)
                    for dch in range(4):
                        ttdn0[ic] = P.do("t", lambda e, ic=ic, buf=buf, dch=dch: e.matmul(
                            dbanks[dch][0:128, 0:512],
                            A_fu[:, ic, 0:128],
                            buf[:, dch * 512:(dch + 1) * 512],
                            start=(ic == 0), stop=(ic == 63)),
                            waits=([t_wd, tv_f[ic], war_downbanks[dch]]
                                   if ic == 0 else ([t_wd, tv_f[ic]] if dch == 0 else [])),
                            inc=(dch == 3))

                ts_g = {}
                ts_u = {}
                tt_g = {}
                tt_u = {}
                tv_f = {}
                for it in range(64):
                    wb = it % 2
                    wu_extra = [t_qkv_done[1]] if (p == 0 and grp == 0 and it < 2) else []
                    if it < 2:
                        wu_extra = wu_extra + [war_wu_reg[wb]]
                    t_wg = P.do("g", lambda e, it=it, wb=wb: e.dma_start(
                        out=wgb[wb][:], in_=wg_d[it]),
                        waits=[war_wbuf[wb]] +
                              ([war_wg_reg[wb]] if it < 2 else []),
                        stream="wd%d" % (4 + wb) if False else "wg%d" % wb)
                    t_wu = P.do("g", lambda e, it=it, wb=wb: e.dma_start(
                        out=wub[wb][:], in_=wu_d[it]),
                        waits=[war_wbuf[wb]] + wu_extra, stream="wu%d" % wb)
                    ttg = None
                    for c in range(16):
                        ttg = P.do("t", lambda e, it=it, wb=wb, c=c: e.matmul(
                            pA[0:128, 0:256], wgb[wb][:, c, :], ln2_fm[:, c, 0:256],
                            start=(c == 0), stop=(c == 15)),
                            waits=[t_wg, t_ln2,
                                   ts_g.get(it - 1, carry_tsg)] if c == 0 else [],
                            inc=(c == 15))
                    tt_g[it] = ttg
                    ttu = None
                    for c in range(16):
                        ttu = P.do("t", lambda e, it=it, wb=wb, c=c: e.matmul(
                            pB[0:128, 0:256], wub[wb][:, c, :], ln2_fm[:, c, 0:256],
                            start=(c == 0), stop=(c == 15)),
                            waits=[t_wu, ts_u.get(it - 1, carry_tsu)] if c == 0 else [],
                            inc=(c == 15))
                    tt_u[it] = ttu
                    war_wbuf[wb] = ttu
                    ts_g[it] = P.do("s", lambda e, it=it, wb=wb: e.activation(
                        gtmp[:, wb, :], pA[0:128, 0:256], AF.Silu,
                        scale=sg_s[:, it:it + 1]),
                        waits=[ttg, war_gup[wb], t_consts], relax=True, inc=True)
                    ts_u[it] = P.do("s", lambda e, it=it, wb=wb: e.activation(
                        utmp[:, wb, :], pB[0:128, 0:256], AF.Copy,
                        scale=su_s[:, it:it + 1]),
                        waits=[ttu, war_gup[wb]], relax=True, inc=True)
                    tv_f[it] = P.do("v", lambda e, it=it, wb=wb: e.tensor_tensor(
                        A_fu[:, it, :], gtmp[:, wb, :], utmp[:, wb, :], ALU.mult),
                        waits=[ts_g[it], ts_u[it],
                               t_comb_done_t if it < 2 else None],
                        relax=True, inc=True)
                    war_gup[wb] = tv_f[it]
                    if it >= 2:
                        emit_down0(it - 2)
                emit_down0(62)
                emit_down0(63)
                t_fuse_all = tv_f[63]
                carry_tsg = ts_g[63]
                carry_tsu = ts_u[63]
                if grp == 0:
                    t_ln2_next = emit_ln2(1, tt_u[63])
                loc = grp * 2
                t_last_dn = ttdn0[63]
                tvy = None
                for dch in range(4):
                    tvd2 = P.do("v", lambda e, dch=dch: e.tensor_tensor(
                        dtmp, dbanks[dch][0:128, 0:512],
                        sdb[:, dch * 512:(dch + 1) * 512], ALU.mult),
                        waits=[t_last_dn, t_sdb], inc=True)
                    tvy = P.do("v", lambda e, loc=loc, dch=dch: e.tensor_add(
                        x1[:, loc, dch * 512:(dch + 1) * 512],
                        x1[:, loc, dch * 512:(dch + 1) * 512], dtmp), inc=True)
                    war_downbanks[dch] = tvy
                t_y[loc] = P.do("y", lambda e, p=p, loc=loc: e.dma_start(
                    out=y_d[p * 512 + loc * 128: p * 512 + (loc + 1) * 128, :],
                    in_=x1[:, loc, :]), waits=[tvy], stream="o%d" % loc)
                wd_slots = [wdb[0], wdb[1], wub_flat[0], wub_flat[1],
                            wgb_flat[0], wgb_flat[1]]
                war_wd6 = [ttdn0[62], ttdn0[63], tt_u[62], tt_u[63],
                           tt_g[62], tt_g[63]]
                loc = grp * 2 + 1
                ttdn = {}
                for ic in range(64):
                    sl = ic % 6
                    buf = wd_slots[sl]
                    t_wd = P.do("g", lambda e, ic=ic, buf=buf: e.dma_start(
                        out=buf[:], in_=wd_d[ic * 128:(ic + 1) * 128, :]),
                        waits=[ttdn.get(ic - 6), war_wd6[sl] if ic < 6 else None],
                        stream="wd%d" % sl)
                    for dch in range(4):
                        ttdn[ic] = P.do("t", lambda e, ic=ic, buf=buf, dch=dch: e.matmul(
                            dbanks[dch][0:128, 0:512],
                            A_fu[:, ic, 128:256],
                            buf[:, dch * 512:(dch + 1) * 512],
                            start=(ic == 0), stop=(ic == 63)),
                            waits=([t_wd, t_fuse_all, war_downbanks[dch]]
                                   if ic == 0 else ([t_wd] if dch == 0 else [])),
                            inc=(dch == 3))
                t_last_dn = ttdn[63]
                tvy = None
                for dch in range(4):
                    tvd2 = P.do("v", lambda e, dch=dch: e.tensor_tensor(
                        dtmp, dbanks[dch][0:128, 0:512],
                        sdb[:, dch * 512:(dch + 1) * 512], ALU.mult),
                        waits=[t_last_dn, t_sdb], inc=True)
                    tvy = P.do("v", lambda e, loc=loc, dch=dch: e.tensor_add(
                        x1[:, loc, dch * 512:(dch + 1) * 512],
                        x1[:, loc, dch * 512:(dch + 1) * 512], dtmp), inc=True)
                    war_downbanks[dch] = tvy
                t_y[loc] = P.do("y", lambda e, p=p, loc=loc: e.dma_start(
                    out=y_d[p * 512 + loc * 128: p * 512 + (loc + 1) * 128, :],
                    in_=x1[:, loc, :]), waits=[tvy], stream="o%d" % loc)
                war_wd = [ttdn[60], ttdn[61]]
                war_wu_reg = [ttdn[62], ttdn[63]]
                war_wg_reg = [ttdn[58], ttdn[59]]
                war_ln2 = tt_u[63]
            war_arena_p2 = t_last_dn

        for loc in range(4):
            P.raw_wait("y", "o%d" % loc, P.cnt["o%d" % loc])

        globals()["LAST_PLAN"] = P

        def runner(stream):
            def run(eng):
                for item in stream:
                    if item[0] == "w":
                        eng.wait_ge(sems[item[1]], item[2])
                    elif item[0] == "i":
                        item[1](eng).then_inc(sems[item[2]], STEP[item[2]])
                    else:
                        item[1](eng)
            return run

        with nc.Block() as block:
            block.tensor(runner(P.ops["t"]))
            block.vector(runner(P.ops["v"]))
            block.scalar(runner(P.ops["s"]))
            block.sync(runner(P.ops["y"]))
            block.gpsimd(runner(P.ops["g"]))
    return nc


_NC_CACHE = []


def kernel(L0_attn_x, L0_attn_Wq, L0_attn_Wk, L0_attn_Wv, L0_attn_memory,
           L0_attn_memory_norm, L0_attn_memory_weight, L0_ffn_w_gate,
           L0_ffn_w_up, L0_ffn_w_down, L0_ffn_s_gate, L0_ffn_s_up,
           L0_ffn_s_down):
    x = np.asarray(L0_attn_x, np.float32)
    Wq = np.asarray(L0_attn_Wq, np.float32)
    Wk = np.asarray(L0_attn_Wk, np.float32)
    Wv = np.asarray(L0_attn_Wv, np.float32)
    memory = np.asarray(L0_attn_memory, np.float32)
    mnorm = np.asarray(L0_attn_memory_norm, np.float32)
    mw = np.asarray(L0_attn_memory_weight, np.float32)
    wg8 = np.asarray(L0_ffn_w_gate, np.int8)
    wu8 = np.asarray(L0_ffn_w_up, np.int8)
    wd8 = np.asarray(L0_ffn_w_down, np.int8)
    sg = np.asarray(L0_ffn_s_gate, np.float32)
    su = np.asarray(L0_ffn_s_up, np.float32)
    sd = np.asarray(L0_ffn_s_down, np.float32)

    wqt = np.ascontiguousarray(
        Wq.T.reshape(16, 128, 16, 128).transpose(2, 1, 0, 3).reshape(16, 128, 2048))
    wkt = np.ascontiguousarray(
        Wk.T.reshape(16, 128, 4, 128).transpose(2, 1, 0, 3).reshape(4, 128, 2048))
    wvt = np.ascontiguousarray(
        Wv.T.reshape(16, 128, 512).transpose(1, 0, 2).reshape(128, 8192))
    memt = np.ascontiguousarray(
        memory.reshape(4, 128, 512).transpose(1, 0, 2).reshape(128, 2048))
    mnt = np.ascontiguousarray(mnorm[:, 0].reshape(4, 128).T)
    mwb = np.ascontiguousarray(np.broadcast_to(mw.reshape(1, 1), (128, 1)))
    wgt = np.ascontiguousarray(
        wg8.reshape(16, 128, 64, 128).transpose(2, 1, 0, 3).reshape(64, 128, 2048))
    wut = np.ascontiguousarray(
        wu8.reshape(16, 128, 64, 128).transpose(2, 1, 0, 3).reshape(64, 128, 2048))
    sgt = np.ascontiguousarray(sg.reshape(64, 128).T)
    sut = np.ascontiguousarray(su.reshape(64, 128).T)
    sdb = np.ascontiguousarray(np.broadcast_to(sd.reshape(1, 2048), (128, 2048)))
    eyeb = np.eye(128, dtype=BF)
    eyef = np.eye(128, dtype=np.float32)
    blk2 = np.zeros((128, 2), BF)
    blk2[0:64, 0] = 1
    blk2[64:128, 1] = 1
    ones2 = np.ones((2, 128), BF)
    mask2 = np.zeros((2, 32), np.float32)
    mask2[0, 0::2] = 1.0
    mask2[1, 1::2] = 1.0
    onesf = np.ones((128, 1), np.float32)

    if not _NC_CACHE:
        _NC_CACHE.append(build_nc())
    nc = _NC_CACHE[0]

    in_maps = []
    for core in range(8):
        b, h = core // 2, core % 2
        xb = x[b]
        if h == 0:
            xc = xb
        else:
            xc = np.concatenate([xb[1024:2048], xb[0:1024]], axis=0)
        in_maps.append(dict(
            x=np.ascontiguousarray(xc), wqt=wqt, wkt=wkt, wvt=wvt, memi=memt,
            mnorm=mnt, mw=mwb, wg8=wgt, wu8=wut, wd8=wd8, sg=sgt, su=sut,
            sdb=sdb, eyeb=eyeb, eyef=eyef, blk2=blk2, ones2=ones2,
            mask2=mask2, onesf=onesf))

    res = run_bass_kernel_spmd(nc, in_maps, core_ids=list(range(8)))
    out = np.empty((4, 2048, 2048), np.float32)
    for core in range(8):
        b, h = core // 2, core % 2
        out[b, h * 1024:(h + 1) * 1024, :] = np.asarray(res.results[core]["y"],
                                                        np.float32)
    return out


# version 33 (final)
# speedup vs baseline: 1.0138x; 1.0138x over previous
import numpy as np
import ml_dtypes
import concourse.bass as bass
import concourse.mybir as mybir
from concourse.bass_utils import run_bass_kernel_spmd

F32 = mybir.dt.float32
BF16 = mybir.dt.bfloat16
I8 = mybir.dt.int8
AF = mybir.ActivationFunctionType
ALU = mybir.AluOpType
AX = mybir.AxisListType
BF = ml_dtypes.bfloat16

D = 2048
DKV = 512
NH = 8
NQ = 4
DH = 64
INTER = 8192
EPS = 1e-5
RD = 1.0 / 2048.0

ENGS = ("t", "v", "s", "y", "g")
ENG_SEM = {"t": "t", "v": "v", "s": "s", "g": "gc"}
DMA_STREAMS = ("c", "m", "wv", "xs0", "xs1", "a0", "a1", "a2", "a3", "sg",
               "o0", "o1", "o2", "o3", "wg0", "wg1", "wu0", "wu1",
               "wd0", "wd1", "wd2", "wd3", "wd4", "wd5", "r0", "r1")
ALL_SEMS = ("t", "v", "s", "gc") + DMA_STREAMS
STEP = {k: (1 if k in ("t", "v", "s", "gc") else 16) for k in ALL_SEMS}


class Plan:
    def __init__(self):
        self.ops = {e: [] for e in ENGS}
        self.cnt = {k: 0 for k in ALL_SEMS}
        self.lastw = {e: {} for e in ENGS}

    def do(self, e, fn, waits=(), inc=True, stream=None, relax=False):
        sname = stream if stream is not None else ENG_SEM.get(e)
        if stream is None and e != "t" and not relax and self.cnt[sname] > 0:
            self.ops[e].append(("w", sname, self.cnt[sname]))
            self.lastw[e][sname] = self.cnt[sname]
        for tok in waits:
            if not tok:
                continue
            ws, wv = tok
            if self.lastw[e].get(ws, 0) >= wv:
                continue
            self.lastw[e][ws] = wv
            self.ops[e].append(("w", ws, wv))
        self.cnt[sname] += STEP[sname]
        self.ops[e].append(("i", fn, sname))
        return (sname, self.cnt[sname])

    def raw_wait(self, e, sname, val):
        self.ops[e].append(("w", sname, val))


def build_nc():
    nc = bass.Bass("TRN2", target_bir_lowering=False)

    x_d = nc.dram_tensor("x", [2048, D], F32, kind="ExternalInput").ap()
    wqt_d = nc.dram_tensor("wqt", [16, 128, 2048], F32, kind="ExternalInput").ap()
    wkt_d = nc.dram_tensor("wkt", [4, 128, 2048], F32, kind="ExternalInput").ap()
    wvt_d = nc.dram_tensor("wvt", [128, 8192], F32, kind="ExternalInput").ap()
    mem_d = nc.dram_tensor("memi", [128, 2048], F32, kind="ExternalInput").ap()
    mnorm_d = nc.dram_tensor("mnorm", [128, 4], F32, kind="ExternalInput").ap()
    mw_d = nc.dram_tensor("mw", [128, 1], F32, kind="ExternalInput").ap()
    wg_d = nc.dram_tensor("wg8", [64, 128, 2048], I8, kind="ExternalInput").ap()
    wu_d = nc.dram_tensor("wu8", [64, 128, 2048], I8, kind="ExternalInput").ap()
    wd_d = nc.dram_tensor("wd8", [INTER, D], I8, kind="ExternalInput").ap()
    sg_d = nc.dram_tensor("sg", [128, 64], F32, kind="ExternalInput").ap()
    su_d = nc.dram_tensor("su", [128, 64], F32, kind="ExternalInput").ap()
    sdb_d = nc.dram_tensor("sdb", [128, 2048], F32, kind="ExternalInput").ap()
    eyeb_d = nc.dram_tensor("eyeb", [128, 128], BF16, kind="ExternalInput").ap()
    eyef_d = nc.dram_tensor("eyef", [128, 128], F32, kind="ExternalInput").ap()
    blk2_d = nc.dram_tensor("blk2", [128, 2], BF16, kind="ExternalInput").ap()
    ones2_d = nc.dram_tensor("ones2", [2, 128], BF16, kind="ExternalInput").ap()
    mask2_d = nc.dram_tensor("mask2", [2, 32], F32, kind="ExternalInput").ap()
    onesf_d = nc.dram_tensor("onesf", [128, 1], F32, kind="ExternalInput").ap()
    y_d = nc.dram_tensor("y", [1024, D], F32, kind="ExternalOutput").ap()

    import contextlib
    es = contextlib.ExitStack()
    with es:
        sbt = lambda n, shp, dt: es.enter_context(nc.sbuf_tensor(n, shp, dt))
        eyeb = sbt("eyeb_s", [128, 128], BF16)
        eyef = sbt("eyef_s", [128, 128], F32)
        blk2 = sbt("blk2_s", [128, 2], BF16)
        ones2 = sbt("ones2_s", [2, 128], BF16)
        mask2 = sbt("mask2_s", [2, 32], F32)
        onesf = sbt("onesf_s", [128, 1], F32)
        mem_s = sbt("mem_s", [128, 2048], BF16)
        mnorm = sbt("mnorm_s", [128, 4], BF16)
        mw_s = sbt("mw_s", [128, 1], F32)
        wcol = sbt("wcol", [128, 2], F32)
        sg_s = sbt("sg_s", [128, 64], F32)
        su_s = sbt("su_s", [128, 64], F32)
        bias_s = sbt("bias_s", [128, 32], F32)
        stat = sbt("stat", [128, 16], F32)
        cq = sbt("cq", [2, 16], F32)
        ck = sbt("ck", [2, 4], F32)
        cm = sbt("cm", [2, 16], F32)
        cm2 = sbt("cm2", [2, 16], F32)
        nmax = sbt("nmax", [2, 2], F32)
        rhs2 = sbt("rhs2", [2, 32], BF16)
        denr = sbt("denr", [128, 16], F32)
        rcpd = sbt("rcpd", [128, 128], F32)
        rsrow = sbt("rsrow", [128, 1024], F32)
        rsall = sbt("rsall", [128, 512], F32)
        x1 = sbt("x1", [128, 4, 2048], F32)
        arena = sbt("arena", [128, 16384], BF16)
        q_fm = sbt("q_fm", [128, 16, 1024], BF16)
        k_fm = sbt("k_fm", [128, 4, 2048], BF16)
        v_tm = sbt("v_tm", [128, 16, 8, 66], BF16)
        warena = sbt("warena", [128, 12288], BF16)
        xstage = sbt("xstage", [128, 2, 2048], F32)
        lnstage = sbt("lnstage", [128, 2, 2048], BF16)
        ln2_fm = sbt("ln2_fm", [128, 16, 256], BF16)
        sqg = sbt("sqg", [128, 4, 512], BF16)
        sqs = sbt("sqs", [128, 1024], BF16)
        stg = sbt("stg", [128, 512], BF16)
        gtmp = sbt("gtmp", [128, 2, 256], BF16)
        utmp = sbt("utmp", [128, 2, 256], BF16)

        pst = lambda n, shp, dt: es.enter_context(nc.psum_tensor(n, shp, dt))
        pAB = pst("pAB", [128, 1024], F32)
        pCD = pst("pCD", [128, 1024], F32)
        pA = pAB[:, 0:512]
        pB = pAB[:, 512:1024]
        pC = pCD[:, 0:512]
        pD = pCD[:, 512:1024]
        pE = pst("pE", [128, 512], F32)
        pF = pst("pF", [128, 512], F32)
        pT0 = pst("pT0", [128, 1024], BF16)
        pT1 = pst("pT1", [128, 1024], BF16)

        sems = {k: es.enter_context(nc.semaphore("sem_" + k))
                for k in ALL_SEMS}

        mem4 = mem_s[:].rearrange("p (c f) -> p c f", f=512)
        A_ln = arena[:].rearrange("p (c t) -> p c t", t=1024)
        A_at = arena[:, 0:8192].rearrange("p (c s) -> p c s", s=512)
        A_et = arena[:, 8192:16384].rearrange("p (r s) -> p r s", s=512)
        A_etm = arena[:, 8192:16384].rearrange("p (r s) -> p r s", s=1024)
        A_mm = arena[:, 8192:16384].rearrange("p (c s) -> p c s", s=512)
        A_fu = arena[:].rearrange("p (i s) -> p i s", s=256)
        dtmp = xstage[:, 0, 0:512]
        wgb = [warena[:, b * 2048:(b + 1) * 2048]
               .rearrange("p (c m) -> p c m", m=128) for b in range(2)]
        wub = [warena[:, 4096 + b * 2048:4096 + (b + 1) * 2048]
               .rearrange("p (c m) -> p c m", m=128) for b in range(2)]
        wdb = [warena[:, 8192 + b * 2048:8192 + (b + 1) * 2048] for b in range(2)]
        wub_flat = [warena[:, 4096 + b * 2048:4096 + (b + 1) * 2048] for b in range(2)]
        wgb_flat = [warena[:, b * 2048:(b + 1) * 2048] for b in range(2)]
        wvb = warena[:, 4096:12288].rearrange("p (c f) -> p c f", f=512)

        P = Plan()
        qkv_banks = [pA, pB, pC, pD]

        tslot_banks = [pT0, pT1]
        tslot_war = [None] * 16
        tslot_idx = [0]

        def tr_slot():
            si = tslot_idx[0] % 2
            tslot_idx[0] += 1
            bank = tslot_banks[si]
            return si, bank[:, 0:128]

        t_consts = None
        for dst, src in ((eyeb, eyeb_d), (eyef, eyef_d), (blk2, blk2_d),
                         (ones2, ones2_d), (mask2, mask2_d), (onesf, onesf_d),
                         (sg_s, sg_d), (su_s, su_d), (mw_s, mw_d)):
            t_consts = P.do("y", lambda e, d=dst, s=src: e.dma_start(out=d[:], in_=s[:]), stream="c")
        t_mem = P.do("g", lambda e: e.dma_start(out=mem_s[:], in_=mem_d[:]), stream="m")
        t_mem = P.do("g", lambda e: e.dma_start(out=mnorm[:], in_=mnorm_d[:]), stream="m")
        t_wv = P.do("g", lambda e: e.dma_start(out=wvb[:], in_=wvt_d[:].rearrange("p (c f) -> p c f", f=512)), stream="wv")

        t_w1 = P.do("s", lambda e: e.activation(wcol[:, 0:1], mw_s[:], AF.Sigmoid),
                    waits=[t_consts], inc=True)
        t_w2 = P.do("v", lambda e: e.tensor_scalar(out=wcol[:, 1:2], in0=wcol[:, 0:1],
                    scalar1=-1.0, scalar2=1.0, op0=ALU.mult, op1=ALU.add),
                    waits=[t_w1], inc=True)
        t_vones = P.do("v", lambda e: e.memset(v_tm[:, :, :, 64:66], 1.0), inc=True)
        P.do("v", lambda e: e.memset(rsall[:], 1.0))
        P.do("v", lambda e: e.memset(ck[:], 0.0))

        def emit_ln_stats(src_ap, slot, waits_in, war_lnstage, sum_on_s=False):
            b = slot * 8
            s_ = lambda k: stat[:, b + k:b + k + 1]
            if sum_on_s:
                tv1 = P.do("s", lambda e: e.activation(x1[:, 0, :], src_ap,
                           AF.Copy, accum_out=s_(0)),
                           waits=waits_in + [war_lnstage], inc=True)
                P.lastj = tv1
            else:
                tv1 = P.do("v", lambda e: e.reduce_sum(s_(0), src_ap, axis=AX.X),
                           waits=waits_in + [war_lnstage], relax=True, inc=True)
            ts1 = P.do("s", lambda e: e.activation(lnstage[:, slot, :], src_ap,
                       AF.Square, accum_out=s_(1)),
                       waits=waits_in + [war_lnstage], relax=True, inc=True)
            ta = P.do("v", lambda e: e.tensor_scalar(out=s_(2), in0=s_(0),
                      scalar1=-RD, scalar2=None, op0=ALU.mult),
                      waits=[tv1, ts1], relax=True, inc=True)
            tb = P.do("v", lambda e: e.tensor_tensor(s_(3), s_(2), s_(2), ALU.mult),
                      waits=[ta], relax=True, inc=True)
            tc_ = P.do("v", lambda e: e.tensor_scalar(out=s_(4), in0=s_(1),
                       scalar1=RD, scalar2=EPS, op0=ALU.mult, op1=ALU.add),
                       waits=[ts1], relax=True, inc=True)
            td = P.do("v", lambda e: e.tensor_sub(s_(5), s_(4), s_(3)),
                      waits=[tb, tc_], relax=True, inc=True)
            tv2 = P.do("v", lambda e: e.reciprocal(s_(4), s_(5)),
                       waits=[td], relax=True, inc=True)
            ts2 = P.do("s", lambda e: e.activation(s_(6), s_(4), AF.Sqrt),
                       waits=[tv2], relax=True, inc=True)
            tv3 = P.do("v", lambda e: e.tensor_tensor(s_(7), s_(2), s_(6), ALU.mult),
                       waits=[ts2], relax=True, inc=True)
            return tv3

        def emit_ln_apply(src_ap, slot, tv3, eng="v"):
            b = slot * 8
            tap = P.do(eng, lambda e: e.tensor_scalar(out=lnstage[:, slot, :],
                       in0=src_ap, scalar1=stat[:, b + 6:b + 7],
                       scalar2=stat[:, b + 7:b + 8],
                       op0=ALU.mult, op1=ALU.add), waits=[tv3], inc=True)
            return tap

        def emit_ln(src_ap, slot, waits_in, war_lnstage):
            tv3 = emit_ln_stats(src_ap, slot, waits_in, war_lnstage)
            return emit_ln_apply(src_ap, slot, tv3)

        def emit_tr16(slot, dst_fn, t_apply, extra_read_war, bank_war=()):
            tv_last = None
            for c in range(16):
                si, ap = tr_slot()
                tt = P.do("t", lambda e, c=c, ap=ap: e.transpose(
                    ap, lnstage[:, slot, c * 128:(c + 1) * 128], eyeb[:]),
                    waits=[t_apply, tslot_war[si]] + list(bank_war), inc=True)
                tv = P.do("v", lambda e, c=c, ap=ap: e.tensor_copy(dst_fn(c), ap),
                          waits=[tt] + ([extra_read_war] if c < 2 else []),
                          relax=True, inc=True)
                tslot_war[si] = tv
                tv_last = tv
            return tv_last

        t_x = [None] * 16
        t_tile = [None] * 16
        war_xstage = [None, None]
        war_lnst = [None, None]
        t_half = [None, None]
        t_qkv_done = [None, None]
        qkv_war = [None, None, None, None]
        t_q_evac = {}
        t_k_evac = {}

        def norm_q_chunk(j):
            tvq = P.do("v", lambda e, j=j: e.tensor_tensor(
                sqs[:], q_fm[:, j, :], q_fm[:, j, :], ALU.mult),
                waits=[t_q_evac[j]], inc=True)
            for hl in range(2):
                tt = P.do("t", lambda e, hl=hl: e.matmul(
                    pF[0:2, 0:512], blk2[:], sqs[:, hl * 512:(hl + 1) * 512],
                    start=True, stop=True), waits=[tvq], inc=True)
                tvq = P.do("v", lambda e, hl=hl: e.reduce_max(
                    nmax[:, hl:hl + 1], pF[0:2, 0:512], axis=AX.X),
                    waits=[tt], inc=True)
            return P.do("v", lambda e, j=j: e.tensor_max(
                cq[:, j:j + 1], nmax[:, 0:1], nmax[:, 1:2]), inc=True)

        def norm_k_chunk(j2, H, t_evac):
            tok = t_evac
            for hl in range(2):
                tvq = P.do("v", lambda e, j2=j2, H=H, hl=hl: e.tensor_tensor(
                    sqs[:, 0:512], k_fm[:, j2, H * 1024 + hl * 512: H * 1024 + (hl + 1) * 512],
                    k_fm[:, j2, H * 1024 + hl * 512: H * 1024 + (hl + 1) * 512], ALU.mult),
                    waits=[tok], inc=True)
                tt = P.do("t", lambda e: e.matmul(
                    pF[0:2, 0:512], blk2[:], sqs[:, 0:512], start=True, stop=True),
                    waits=[tvq], inc=True)
                tvq = P.do("v", lambda e: e.reduce_max(
                    nmax[:, 0:1], pF[0:2, 0:512], axis=AX.X), waits=[tt], inc=True)
                tok = P.do("v", lambda e, j2=j2: e.tensor_max(
                    ck[:, j2:j2 + 1], ck[:, j2:j2 + 1], nmax[:, 0:1]), inc=True)
            return tok

        nb = [0]

        def qkv_group(lhs_fn, rhs_fn, evac_fn, nK, waits_mm):
            b = nb[0] % 4
            nb[0] += 1
            bank = qkv_banks[b]
            tt = None
            for c in range(nK):
                tt = P.do("t", lambda e, c=c, bank=bank: e.matmul(
                    bank[:, 0:512], lhs_fn(c), rhs_fn(c),
                    start=(c == 0), stop=(c == nK - 1)),
                    waits=(waits_mm + [qkv_war[b]]) if c == 0 else (), inc=(c == nK - 1))
            tv = P.do("v", lambda e, bank=bank: evac_fn(e, bank), waits=[tt], inc=True)
            qkv_war[b] = tv
            return tv

        war_wbuf = [t_wv, t_wv]
        t_wq = {}

        t_k0_done = None

        def emit_tile_tail(i):
            sl = i % 2
            i8 = i % 8
            extra = t_k0_done if i >= 8 else None
            tvl = emit_tr16(sl, lambda c, i8=i8: A_ln[:, c, i8 * 128:(i8 + 1) * 128],
                            t_apply[i], extra)
            war_lnst[sl] = tvl
            t_tile[i] = tvl
            qkv_group(
                lambda c, i8=i8: A_ln[:, c, i8 * 128:(i8 + 1) * 128],
                lambda c: wvb[:, c, 0:512],
                lambda e, bank, i=i: e.tensor_copy(
                    v_tm[:, i, :, 0:64],
                    bank[:, 0:512].rearrange("p (h d) -> p h d", d=64)),
                16, [t_wv, t_tile[i], t_vones])

        t_apply = [None] * 16
        for H in range(2):
            for i8 in range(8):
                i = H * 8 + i8
                sl = i % 2
                t_x[i] = P.do("y", lambda e, i=i, sl=sl: e.dma_start(
                    out=xstage[:, sl, :], in_=x_d[i * 128:(i + 1) * 128, :]),
                    waits=[war_xstage[sl]], stream="xs%d" % sl)
                tv3 = emit_ln_stats(xstage[:, sl, :], sl, [t_x[i]], war_lnst[sl],
                                    sum_on_s=True)
                if i8 > 0:
                    emit_tile_tail(i - 1)
                t_apply[i] = emit_ln_apply(xstage[:, sl, :], sl, tv3, eng="g")
                war_xstage[sl] = t_apply[i]
            emit_tile_tail(H * 8 + 7)
            t_half[H] = t_tile[H * 8 + 7]

            if H == 0:
                for j in range(16):
                    wb = j % 2
                    t_wq[j] = P.do("g", lambda e, j=j, wb=wb: e.dma_start(
                        out=wgb[wb][:], in_=wqt_d[j].rearrange("p (c m) -> p c m", m=128)),
                        waits=[war_wbuf[wb]], stream="wg%d" % wb)
                    for sc in range(2):
                        tv = qkv_group(
                            lambda c, j=j, wb=wb: wgb[wb][:, c, :],
                            lambda c, sc=sc: A_ln[:, c, sc * 512:(sc + 1) * 512],
                            lambda e, bank, j=j, sc=sc: e.tensor_copy(
                                q_fm[:, j, sc * 512:(sc + 1) * 512], bank[:, 0:512]),
                            16, [t_wq[j], t_tile[4 * sc + 3]])
                    t_q_evac[j] = tv
                    war_wbuf[wb] = ("t", P.cnt["t"])
                    norm_q_chunk(j)
            for j2 in range(4):
                wb = j2 % 2
                t_wk = P.do("g", lambda e, j2=j2, wb=wb: e.dma_start(
                    out=wgb[wb][:], in_=wkt_d[j2].rearrange("p (c m) -> p c m", m=128)),
                    waits=[war_wbuf[wb]], stream="wg%d" % wb)
                for tcH in range(2):
                    tv = qkv_group(
                        lambda c, wb=wb: wgb[wb][:, c, :],
                        lambda c, tcH=tcH: A_ln[:, c, tcH * 512:(tcH + 1) * 512],
                        lambda e, bank, j2=j2, H=H, tcH=tcH: e.tensor_copy(
                            k_fm[:, j2, H * 1024 + tcH * 512: H * 1024 + (tcH + 1) * 512],
                            bank[:, 0:512]),
                        16, [t_wk, t_tile[H * 8 + tcH * 4 + 3]])
                t_k_evac[(j2, H)] = tv
                war_wbuf[wb] = ("t", P.cnt["t"])
                norm_k_chunk(j2, H, tv)
            if H == 0:
                t_k0_done = ("t", P.cnt["t"])
            t_qkv_done[H] = ("t", P.cnt["t"])

        tvb = None
        for r4 in range(4):
            tvb = P.do("v", lambda e, r4=r4: e.tensor_tensor(
                cm[:, r4 * 4:(r4 + 1) * 4], cq[:, r4 * 4:(r4 + 1) * 4],
                ck[:, 0:4], ALU.mult), inc=True)
        tsb = P.do("s", lambda e: e.activation(cm2[:], cm[:], AF.Sqrt, scale=1.0 / 64.0),
                   waits=[tvb], inc=True)
        rhs2v = rhs2[:].rearrange("p (c u) -> p c u", u=2)
        cm2v = cm2[:].rearrange("a (b o) -> a b o", o=1)
        P.do("v", lambda e: e.tensor_copy(rhs2v[:, :, 0:1], cm2v[:]), waits=[tsb])
        P.do("v", lambda e: e.tensor_copy(rhs2v[:, :, 1:2], cm2v[:]))
        tvb = P.do("v", lambda e: e.tensor_tensor(rhs2[:], rhs2[:], mask2[:],
                   ALU.mult), waits=[t_consts])
        ttb = P.do("t", lambda e: e.matmul(pF[0:128, 32:64], ones2[:], rhs2[:],
                   start=True, stop=True), waits=[tvb], inc=True)
        t_bias = P.do("s", lambda e: e.activation(bias_s[:], pF[:, 32:64], AF.Copy,
                      scale=-1.0), waits=[ttb], inc=True)

        t_sdb = P.do("y", lambda e: e.dma_start(out=xstage[:, 1, :], in_=sdb_d[:]),
                     waits=[war_xstage[1]], stream="xs1")
        sdb = xstage[:, 1, :]

        war_arena_p2 = t_qkv_done[1]
        war_stg = None
        tvd1 = None
        war_sqg = None
        war_sqs_p2 = None
        war_downbanks = [None, None, None, None]
        war_gup = [None, None]
        war_lnst2 = list(war_lnst)
        war_ln2 = None
        war_wd = [t_qkv_done[1], t_qkv_done[1]]
        war_wu_reg = [None, None]
        war_wg_reg = [None, None]
        t_y = [None] * 4

        for p in range(2):
            if p == 0:
                war_sb_init = [[qkv_war[0], qkv_war[2]], [qkv_war[1], qkv_war[3]]]
                war_av = [None, t_bias]
                war_rsall = None
            else:
                war_sb_init = [[carry_tsg, war_downbanks[0], war_memb,
                                t_rcp_sub[3]],
                               [carry_tsu, war_downbanks[1], tvd1, t_rcp_sub[3]]]
                war_av = [war_downbanks[2], war_downbanks[3]]
                war_rsall = tt_rstr

            t_x1 = []
            for loc in range(4):
                t_x1.append(P.do("y", lambda e, p=p, loc=loc: e.dma_start(
                    out=x1[:, loc, :],
                    in_=x_d[p * 512 + loc * 128: p * 512 + (loc + 1) * 128, :]),
                    waits=[t_y[loc]] if p == 1 else
                          ([P.lastj] if loc == 0 else []), stream="a%d" % loc))

            t_exp = {}
            t_av = {}
            t_atA = {}
            t_atB = {}
            sbank = [pAB, pCD]
            avbank = [pE, pF]
            war_rsrow = [war_rsall, war_rsall]
            for i2 in range(16):
                prs = (2 * i2, 2 * i2 + 1)
                kc = (2 * i2) % 8 // 2
                for tc in range(20):
                    if tc >= 4:
                        tca = tc - 4
                        for st in range(2):
                            j = prs[st]
                            t_av[(j, tca)] = P.do(
                                "t", lambda e, j=j, tca=tca, st=st:
                                e.matmul(avbank[st][0:65, 0:512],
                                         v_tm[:, tca, j % 8, 0:65],
                                         A_et[:, st * 8 + tca % 8, :],
                                         start=(tca == 0), stop=(tca == 15)),
                                waits=[t_exp[(j, tca)],
                                       war_av[st] if tca == 0 else None],
                                inc=True)
                    if tc < 16:
                        half = (tc % 2) * 512
                        w0 = [t_exp.get((prs[1], tc - 2)), t_bias,
                              war_arena_p2]
                        if i2 == 0 and tc < 2:
                            w0 += war_sb_init[tc % 2]
                        ttSA = P.do("t", lambda e, i2=i2, kc=kc, tc=tc, half=half, p=p:
                                    e.matmul(pAB[0:128, half:half + 512],
                                             k_fm[0:64, kc, tc * 128:(tc + 1) * 128],
                                             q_fm[0:64, i2, p * 512:(p + 1) * 512],
                                             start=True, stop=True,
                                             tile_position=(0, 0)),
                                    waits=w0, inc=True)
                        ttSB = P.do("t", lambda e, i2=i2, kc=kc, tc=tc, half=half, p=p:
                                    e.matmul(pCD[0:128, half:half + 512],
                                             k_fm[64:128, kc, tc * 128:(tc + 1) * 128],
                                             q_fm[64:128, i2, p * 512:(p + 1) * 512],
                                             start=True, stop=True,
                                             tile_position=(64, 0)),
                                    inc=True)
                        for st in range(2):
                            j = prs[st]
                            wslot = t_av.get((j, tc - 8))
                            if wslot is None and i2 > 0:
                                wslot = t_av.get((j - 2, tc + 8))
                            t_exp[(j, tc)] = P.do(
                                "s", lambda e, j=j, tc=tc, st=st, half=half:
                                e.activation(A_et[:, st * 8 + tc % 8, :],
                                             sbank[st][0:128, half:half + 512],
                                             AF.Exp, bias=bias_s[:, j:j + 1],
                                             scale=0.125),
                                waits=[ttSB, t_bias, wslot,
                                       war_arena_p2], relax=True, inc=True)
                last_avA = t_av[(prs[0], 15)]
                tvdA = P.do("v", lambda e, i2=i2: e.tensor_copy(
                    arena[0:64, i2 * 512:(i2 + 1) * 512], pE[0:64, 0:512]),
                    waits=[last_avA, war_arena_p2], inc=True)
                t_atA[i2] = tvdA
                tvrA = P.do("v", lambda e: e.tensor_copy(
                    rsrow[64:65, 0:512], pE[64:65, 0:512]),
                    waits=[last_avA, war_rsrow[0]], relax=True, inc=True)
                war_av[0] = tvrA
                t_rsA = P.do("y", lambda e, jA=prs[0]: e.dma_start(
                    out=rsall[jA:jA + 1, 0:512], in_=rsrow[64:65, 0:512]),
                    waits=[tvrA, war_rsall], stream="r0")
                war_rsrow[0] = t_rsA
                last_avB = t_av[(prs[1], 15)]
                tvdB = P.do("v", lambda e: e.tensor_copy(
                    stg[0:64, 0:512], pF[0:64, 0:512]),
                    waits=[last_avB, war_stg], inc=True)
                t_atB[i2] = P.do("y", lambda e, i2=i2: e.dma_start(
                    out=arena[64:128, i2 * 512:(i2 + 1) * 512], in_=stg[0:64, 0:512]),
                    waits=[tvdB, war_arena_p2], stream="sg")
                war_stg = t_atB[i2]
                tvrB = P.do("v", lambda e: e.tensor_copy(
                    rsrow[64:65, 512:1024], pF[64:65, 0:512]),
                    waits=[last_avB, war_rsrow[1]], relax=True, inc=True)
                war_av[1] = tvrB
                t_rsB = P.do("y", lambda e, jB=prs[1]: e.dma_start(
                    out=rsall[jB:jB + 1, 0:512], in_=rsrow[64:65, 512:1024]),
                    waits=[tvrB, war_rsall], stream="r1")
                war_rsrow[1] = t_rsB

            war_exp_A = t_exp[(30, 15)]
            war_exp_B = t_exp[(31, 15)]
            rtslots = [pAB[:, 0:128], pAB[:, 512:640],
                       pCD[:, 0:128], pCD[:, 512:640]]
            t_rcp_sub = []
            tt_rstr = None
            for sub in range(4):
                tt_rstr = P.do("t", lambda e, sub=sub: e.transpose(
                    rtslots[sub],
                    rsall[:, sub * 128:(sub + 1) * 128], eyef[:]),
                    waits=[t_rsA, t_rsB, war_exp_A, war_exp_B], inc=True)
                t_rcp_sub.append(P.do("v", lambda e, sub=sub: e.reciprocal(
                    rcpd[:, sub * 32:(sub + 1) * 32],
                    rtslots[sub][:, 0:32]), waits=[tt_rstr], inc=True))
            t_rcpall = P.do("v", lambda e: e.tensor_scalar(
                out=rcpd[:], in0=rcpd[:], scalar1=wcol[:, 1:2], scalar2=None,
                op0=ALU.mult), waits=[t_rcp_sub[3], t_w2], inc=True)

            t_mm_evac = {}
            tt_den_last = None
            last_exp = t_exp[(31, 15)]
            last_avt = t_av[(31, 15)]
            war_memb = t_rcp_sub[3]
            for qg in range(4):
                tv_elu = None
                for ch in range(2):
                    sqsv = sqs[:].rearrange("p (a s) -> p a s", s=512)
                    tvm = P.do("v", lambda e, qg=qg, ch=ch, p=p: e.tensor_scalar(
                        out=sqsv[:], in0=q_fm[:, 4 * qg + 2 * ch: 4 * qg + 2 * ch + 2,
                                             p * 512:(p + 1) * 512],
                        scalar1=0.0, scalar2=None, op0=ALU.min),
                        waits=[war_sqs_p2], inc=True)
                    tse = P.do("s", lambda e, qg=qg, ch=ch: e.activation(
                        sqg[:, 2 * ch:2 * ch + 2, :], sqsv[:], AF.Exp),
                        waits=[tvm, war_sqg], inc=True)
                    tvm2 = P.do("v", lambda e, qg=qg, ch=ch, p=p: e.tensor_scalar(
                        out=sqsv[:], in0=q_fm[:, 4 * qg + 2 * ch: 4 * qg + 2 * ch + 2,
                                             p * 512:(p + 1) * 512],
                        scalar1=0.0, scalar2=None, op0=ALU.max),
                        waits=[tse], inc=True)
                    war_sqs_p2 = tvm2
                    tv_elu = P.do("v", lambda e, ch=ch: e.scalar_tensor_tensor(
                        out=sqg[:, 2 * ch:2 * ch + 2, :], in0=sqg[:, 2 * ch:2 * ch + 2, :],
                        scalar=-1.0, in1=sqsv[:], op0=ALU.add, op1=ALU.add),
                        waits=[tse, tvm2], inc=True)
                for fc in range(4):
                    ttm = None
                    for i4 in range(4):
                        ttm = P.do("t", lambda e, qg=qg, fc=fc, i4=i4: e.matmul(
                            pAB[0:128, 0:512], mem4[:, i4, fc * 128:(fc + 1) * 128],
                            sqg[:, i4, :], start=(i4 == 0), stop=(i4 == 3)),
                            waits=[tv_elu, t_mem, war_memb] if i4 == 0 else [],
                            inc=(i4 == 3))
                    war_memb = P.do("v", lambda e, qg=qg, fc=fc: e.tensor_copy(
                        A_mm[:, qg * 4 + fc, :], pAB[0:128, 0:512]),
                        waits=[ttm, last_avt, last_exp], inc=True)
                    t_mm_evac[qg * 4 + fc] = war_memb
                for sub in range(4):
                    for i4 in range(4):
                        tt_den_last = P.do("t", lambda e, qg=qg, sub=sub, i4=i4: e.matmul(
                            pAB[0:128, 512 + qg * 4 + sub: 512 + qg * 4 + sub + 1],
                            sqg[:, i4, sub * 128:(sub + 1) * 128],
                            mnorm[:, i4:i4 + 1], start=(i4 == 0), stop=(i4 == 3)),
                            waits=[tv_elu, t_mem, war_exp_A, tvd1,
                                   t_rcp_sub[1]] if (sub == 0 and i4 == 0) else [],
                            inc=(i4 == 3))
                war_sqg = tt_den_last
            tvd1 = P.do("v", lambda e: e.reciprocal(denr[:], pAB[:, 512:528]),
                        waits=[tt_den_last], inc=True)
            t_denr = P.do("v", lambda e: e.tensor_scalar(
                out=denr[:], in0=denr[:], scalar1=wcol[:, 0:1], scalar2=None,
                op0=ALU.mult), waits=[t_w2], inc=True)

            tv_stt = {}
            for sub in range(4):
                for c2 in range(16):
                    si, ap = tr_slot()
                    ttd = P.do("t", lambda e, c2=c2, sub=sub, ap=ap: e.transpose(
                        ap, arena[:, c2 * 512 + sub * 128: c2 * 512 + sub * 128 + 128],
                        eyeb[:]),
                        waits=[t_atA[c2], t_atB[c2], tslot_war[si]],
                        inc=True)
                    tva = P.do("v", lambda e, c2=c2, sub=sub, ap=ap: e.scalar_tensor_tensor(
                        out=x1[:, sub, c2 * 128: c2 * 128 + 64], in0=ap[:, 0:64],
                        scalar=rcpd[:, sub * 32 + 2 * c2: sub * 32 + 2 * c2 + 1],
                        in1=x1[:, sub, c2 * 128: c2 * 128 + 64],
                        op0=ALU.mult, op1=ALU.add),
                        waits=[ttd, t_rcpall, t_x1[sub], tv_stt.get((c2 - 1, sub))],
                        relax=True, inc=True)
                    tvb2 = P.do("v", lambda e, c2=c2, sub=sub, ap=ap: e.scalar_tensor_tensor(
                        out=x1[:, sub, c2 * 128 + 64: c2 * 128 + 128], in0=ap[:, 64:128],
                        scalar=rcpd[:, sub * 32 + 2 * c2 + 1: sub * 32 + 2 * c2 + 2],
                        in1=x1[:, sub, c2 * 128 + 64: c2 * 128 + 128],
                        op0=ALU.mult, op1=ALU.add),
                        waits=[ttd, t_rcpall, t_x1[sub], tv_stt.get((c2 - 1, sub))],
                        relax=True, inc=True)
                    tslot_war[si] = tvb2
                    si2, ap2 = tr_slot()
                    ttm2 = P.do("t", lambda e, c2=c2, sub=sub, ap2=ap2: e.transpose(
                        ap2,
                        arena[:, 8192 + c2 * 512 + sub * 128: 8192 + c2 * 512 + sub * 128 + 128],
                        eyeb[:]), waits=[t_mm_evac[c2], tslot_war[si2]],
                        inc=True)
                    tvm3 = P.do("v", lambda e, c2=c2, sub=sub, ap2=ap2: e.scalar_tensor_tensor(
                        out=x1[:, sub, c2 * 128: (c2 + 1) * 128], in0=ap2,
                        scalar=denr[:, (c2 // 4) * 4 + sub: (c2 // 4) * 4 + sub + 1],
                        in1=x1[:, sub, c2 * 128: (c2 + 1) * 128],
                        op0=ALU.mult, op1=ALU.add),
                        waits=[ttm2, t_denr], inc=True)
                    tslot_war[si2] = tvm3
                    tv_stt[(c2, sub)] = tvm3
            t_comb_done = ("v", P.cnt["v"])
            t_comb_done_t = ("t", P.cnt["t"])

            carry_tsg = t_comb_done_t
            carry_tsu = t_comb_done_t

            def emit_ln2(grp, war_ln2_tok):
                t_ln2 = None
                for st2 in range(2):
                    loc = grp * 2 + st2
                    sl = st2 % 2
                    tap = emit_ln(x1[:, loc, :], sl,
                                  [tv_stt[(15, loc)]], war_lnst2[sl])
                    tvl = emit_tr16(sl, lambda c, st2=st2: ln2_fm[:, c, st2 * 128:(st2 + 1) * 128],
                                    tap, war_ln2_tok)
                    war_lnst2[sl] = tvl
                    t_ln2 = tvl
                return t_ln2

            t_ln2_next = emit_ln2(0, war_ln2)
            dbanks = [pC, pD, pE, pF]
            for grp in range(2):
                t_ln2 = t_ln2_next
                ttdn0 = {}
                loc0 = grp * 2

                def emit_down0(ic, grp=grp):
                    sl = ic % 2
                    buf = wdb[sl]
                    t_wd = P.do("g", lambda e, ic=ic, buf=buf: e.dma_start(
                        out=buf[:], in_=wd_d[ic * 128:(ic + 1) * 128, :]),
                        waits=[ttdn0.get(ic - 2), war_wd[sl] if ic < 2 else None],
                        stream="wd%d" % sl)
                    for dch in range(4):
                        ttdn0[ic] = P.do("t", lambda e, ic=ic, buf=buf, dch=dch: e.matmul(
                            dbanks[dch][0:128, 0:512],
                            A_fu[:, ic, 0:128],
                            buf[:, dch * 512:(dch + 1) * 512],
                            start=(ic == 0), stop=(ic == 63)),
                            waits=([t_wd, tv_f[ic], war_downbanks[dch]]
                                   if ic == 0 else ([t_wd, tv_f[ic]] if dch == 0 else [])),
                            inc=(dch == 3))

                ts_g = {}
                ts_u = {}
                tt_g = {}
                tt_u = {}
                tv_f = {}
                for it in range(64):
                    wb = it % 2
                    wu_extra = [t_qkv_done[1]] if (p == 0 and grp == 0 and it < 2) else []
                    if it < 2:
                        wu_extra = wu_extra + [war_wu_reg[wb]]
                    t_wg = P.do("g", lambda e, it=it, wb=wb: e.dma_start(
                        out=wgb[wb][:], in_=wg_d[it]),
                        waits=[war_wbuf[wb]] +
                              ([war_wg_reg[wb]] if it < 2 else []),
                        stream="wd%d" % (4 + wb) if False else "wg%d" % wb)
                    t_wu = P.do("g", lambda e, it=it, wb=wb: e.dma_start(
                        out=wub[wb][:], in_=wu_d[it]),
                        waits=[war_wbuf[wb]] + wu_extra, stream="wu%d" % wb)
                    ttg = None
                    for c in range(16):
                        ttg = P.do("t", lambda e, it=it, wb=wb, c=c: e.matmul(
                            pA[0:128, 0:256], wgb[wb][:, c, :], ln2_fm[:, c, 0:256],
                            start=(c == 0), stop=(c == 15)),
                            waits=[t_wg, t_ln2,
                                   ts_g.get(it - 1, carry_tsg)] if c == 0 else [],
                            inc=(c == 15))
                    tt_g[it] = ttg
                    ttu = None
                    for c in range(16):
                        ttu = P.do("t", lambda e, it=it, wb=wb, c=c: e.matmul(
                            pB[0:128, 0:256], wub[wb][:, c, :], ln2_fm[:, c, 0:256],
                            start=(c == 0), stop=(c == 15)),
                            waits=[t_wu, ts_u.get(it - 1, carry_tsu)] if c == 0 else [],
                            inc=(c == 15))
                    tt_u[it] = ttu
                    war_wbuf[wb] = ttu
                    ts_g[it] = P.do("s", lambda e, it=it, wb=wb: e.activation(
                        gtmp[:, wb, :], pA[0:128, 0:256], AF.Silu,
                        scale=sg_s[:, it:it + 1]),
                        waits=[ttg, war_gup[wb], t_consts], relax=True, inc=True)
                    ts_u[it] = P.do("s", lambda e, it=it, wb=wb: e.activation(
                        utmp[:, wb, :], pB[0:128, 0:256], AF.Copy,
                        scale=su_s[:, it:it + 1]),
                        waits=[ttu, war_gup[wb]], relax=True, inc=True)
                    tv_f[it] = P.do("v", lambda e, it=it, wb=wb: e.tensor_tensor(
                        A_fu[:, it, :], gtmp[:, wb, :], utmp[:, wb, :], ALU.mult),
                        waits=[ts_g[it], ts_u[it],
                               t_comb_done_t if it < 2 else None],
                        relax=True, inc=True)
                    war_gup[wb] = tv_f[it]
                    if it >= 2:
                        emit_down0(it - 2)
                emit_down0(62)
                emit_down0(63)
                t_fuse_all = tv_f[63]
                carry_tsg = ts_g[63]
                carry_tsu = ts_u[63]
                if grp == 0:
                    t_ln2_next = emit_ln2(1, tt_u[63])
                loc = grp * 2
                t_last_dn = ttdn0[63]
                tvy = None
                for dch in range(4):
                    tvd2 = P.do("v", lambda e, dch=dch: e.tensor_tensor(
                        dtmp, dbanks[dch][0:128, 0:512],
                        sdb[:, dch * 512:(dch + 1) * 512], ALU.mult),
                        waits=[t_last_dn, t_sdb], inc=True)
                    tvy = P.do("v", lambda e, loc=loc, dch=dch: e.tensor_add(
                        x1[:, loc, dch * 512:(dch + 1) * 512],
                        x1[:, loc, dch * 512:(dch + 1) * 512], dtmp), inc=True)
                    war_downbanks[dch] = tvy
                t_y[loc] = P.do("y", lambda e, p=p, loc=loc: e.dma_start(
                    out=y_d[p * 512 + loc * 128: p * 512 + (loc + 1) * 128, :],
                    in_=x1[:, loc, :]), waits=[tvy], stream="o%d" % loc)
                wd_slots = [wdb[0], wdb[1], wub_flat[0], wub_flat[1],
                            wgb_flat[0], wgb_flat[1]]
                war_wd6 = [ttdn0[62], ttdn0[63], tt_u[62], tt_u[63],
                           tt_g[62], tt_g[63]]
                loc = grp * 2 + 1
                ttdn = {}
                for ic in range(64):
                    sl = ic % 6
                    buf = wd_slots[sl]
                    t_wd = P.do("g", lambda e, ic=ic, buf=buf: e.dma_start(
                        out=buf[:], in_=wd_d[ic * 128:(ic + 1) * 128, :]),
                        waits=[ttdn.get(ic - 6), war_wd6[sl] if ic < 6 else None],
                        stream="wd%d" % sl)
                    for dch in range(4):
                        ttdn[ic] = P.do("t", lambda e, ic=ic, buf=buf, dch=dch: e.matmul(
                            dbanks[dch][0:128, 0:512],
                            A_fu[:, ic, 128:256],
                            buf[:, dch * 512:(dch + 1) * 512],
                            start=(ic == 0), stop=(ic == 63)),
                            waits=([t_wd, t_fuse_all, war_downbanks[dch]]
                                   if ic == 0 else ([t_wd] if dch == 0 else [])),
                            inc=(dch == 3))
                t_last_dn = ttdn[63]
                tvy = None
                for dch in range(4):
                    tvd2 = P.do("v", lambda e, dch=dch: e.tensor_tensor(
                        dtmp, dbanks[dch][0:128, 0:512],
                        sdb[:, dch * 512:(dch + 1) * 512], ALU.mult),
                        waits=[t_last_dn, t_sdb], inc=True)
                    tvy = P.do("v", lambda e, loc=loc, dch=dch: e.tensor_add(
                        x1[:, loc, dch * 512:(dch + 1) * 512],
                        x1[:, loc, dch * 512:(dch + 1) * 512], dtmp), inc=True)
                    war_downbanks[dch] = tvy
                t_y[loc] = P.do("y", lambda e, p=p, loc=loc: e.dma_start(
                    out=y_d[p * 512 + loc * 128: p * 512 + (loc + 1) * 128, :],
                    in_=x1[:, loc, :]), waits=[tvy], stream="o%d" % loc)
                war_wd = [ttdn[60], ttdn[61]]
                war_wu_reg = [ttdn[62], ttdn[63]]
                war_wg_reg = [ttdn[58], ttdn[59]]
                war_ln2 = tt_u[63]
            war_arena_p2 = t_last_dn

        for loc in range(4):
            P.raw_wait("y", "o%d" % loc, P.cnt["o%d" % loc])

        globals()["LAST_PLAN"] = P

        def runner(stream):
            def run(eng):
                for item in stream:
                    if item[0] == "w":
                        eng.wait_ge(sems[item[1]], item[2])
                    elif item[0] == "i":
                        item[1](eng).then_inc(sems[item[2]], STEP[item[2]])
                    else:
                        item[1](eng)
            return run

        with nc.Block() as block:
            block.tensor(runner(P.ops["t"]))
            block.vector(runner(P.ops["v"]))
            block.scalar(runner(P.ops["s"]))
            block.sync(runner(P.ops["y"]))
            block.gpsimd(runner(P.ops["g"]))
    return nc


_NC_CACHE = []


def kernel(L0_attn_x, L0_attn_Wq, L0_attn_Wk, L0_attn_Wv, L0_attn_memory,
           L0_attn_memory_norm, L0_attn_memory_weight, L0_ffn_w_gate,
           L0_ffn_w_up, L0_ffn_w_down, L0_ffn_s_gate, L0_ffn_s_up,
           L0_ffn_s_down):
    x = np.asarray(L0_attn_x, np.float32)
    Wq = np.asarray(L0_attn_Wq, np.float32)
    Wk = np.asarray(L0_attn_Wk, np.float32)
    Wv = np.asarray(L0_attn_Wv, np.float32)
    memory = np.asarray(L0_attn_memory, np.float32)
    mnorm = np.asarray(L0_attn_memory_norm, np.float32)
    mw = np.asarray(L0_attn_memory_weight, np.float32)
    wg8 = np.asarray(L0_ffn_w_gate, np.int8)
    wu8 = np.asarray(L0_ffn_w_up, np.int8)
    wd8 = np.asarray(L0_ffn_w_down, np.int8)
    sg = np.asarray(L0_ffn_s_gate, np.float32)
    su = np.asarray(L0_ffn_s_up, np.float32)
    sd = np.asarray(L0_ffn_s_down, np.float32)

    wqt = np.ascontiguousarray(
        Wq.T.reshape(16, 128, 16, 128).transpose(2, 1, 0, 3).reshape(16, 128, 2048))
    wkt = np.ascontiguousarray(
        Wk.T.reshape(16, 128, 4, 128).transpose(2, 1, 0, 3).reshape(4, 128, 2048))
    wvt = np.ascontiguousarray(
        Wv.T.reshape(16, 128, 512).transpose(1, 0, 2).reshape(128, 8192))
    memt = np.ascontiguousarray(
        memory.reshape(4, 128, 512).transpose(1, 0, 2).reshape(128, 2048))
    mnt = np.ascontiguousarray(mnorm[:, 0].reshape(4, 128).T)
    mwb = np.ascontiguousarray(np.broadcast_to(mw.reshape(1, 1), (128, 1)))
    wgt = np.ascontiguousarray(
        wg8.reshape(16, 128, 64, 128).transpose(2, 1, 0, 3).reshape(64, 128, 2048))
    wut = np.ascontiguousarray(
        wu8.reshape(16, 128, 64, 128).transpose(2, 1, 0, 3).reshape(64, 128, 2048))
    sgt = np.ascontiguousarray(sg.reshape(64, 128).T)
    sut = np.ascontiguousarray(su.reshape(64, 128).T)
    sdb = np.ascontiguousarray(np.broadcast_to(sd.reshape(1, 2048), (128, 2048)))
    eyeb = np.eye(128, dtype=BF)
    eyef = np.eye(128, dtype=np.float32)
    blk2 = np.zeros((128, 2), BF)
    blk2[0:64, 0] = 1
    blk2[64:128, 1] = 1
    ones2 = np.ones((2, 128), BF)
    mask2 = np.zeros((2, 32), np.float32)
    mask2[0, 0::2] = 1.0
    mask2[1, 1::2] = 1.0
    onesf = np.ones((128, 1), np.float32)

    if not _NC_CACHE:
        _NC_CACHE.append(build_nc())
    nc = _NC_CACHE[0]

    in_maps = []
    for core in range(8):
        b, h = core // 2, core % 2
        xb = x[b]
        if h == 0:
            xc = xb
        else:
            xc = np.concatenate([xb[1024:2048], xb[0:1024]], axis=0)
        in_maps.append(dict(
            x=np.ascontiguousarray(xc), wqt=wqt, wkt=wkt, wvt=wvt, memi=memt,
            mnorm=mnt, mw=mwb, wg8=wgt, wu8=wut, wd8=wd8, sg=sgt, su=sut,
            sdb=sdb, eyeb=eyeb, eyef=eyef, blk2=blk2, ones2=ones2,
            mask2=mask2, onesf=onesf))

    res = run_bass_kernel_spmd(nc, in_maps, core_ids=list(range(8)))
    out = np.empty((4, 2048, 2048), np.float32)
    for core in range(8):
        b, h = core // 2, core % 2
        out[b, h * 1024:(h + 1) * 1024, :] = np.asarray(res.results[core]["y"],
                                                        np.float32)
    return out
